# version 03
# baseline (speedup 1.0000x reference)
import contextlib
import numpy as np
import concourse.bass as bass
import concourse.mybir as mybir
from concourse.bass_utils import run_bass_kernel_spmd

F32 = mybir.dt.float32
BF16 = mybir.dt.bfloat16
I32 = mybir.dt.int32
AF = mybir.ActivationFunctionType
ALU = mybir.AluOpType
AX = mybir.AxisListType

D = 1024
TT = 512
HALO = 4
ENGINES = ("sync", "tensor", "vector", "scalar", "gpsimd")
NEG = -1.0e30
ESIZE = {F32: 4, BF16: 2, I32: 4}


def host_tables(S):
    t = {}
    i = np.arange(128)[:, None]
    j = np.arange(512)[None, :]
    t["maskA"] = np.where((j >= i) & (j < i + 256), 0.0, NEG).astype(np.float32)
    jb = np.arange(640)[None, :]
    base = (jb >= i) & (jb <= i + 512)
    for m, t0 in enumerate((0, 128, 256, 384)):
        t[f"maskB{m}"] = np.where(base & (jb >= 512 - t0), 0.0, NEG).astype(np.float32)
    t["maskB4"] = np.where(base, 0.0, NEG).astype(np.float32)
    DH, WIN, QKS = 64, 512, 8.0
    half = np.arange(0, DH, 2, dtype=np.float64)
    inv_freq = 1.0 / 10000.0 ** (half / DH)
    tt = np.arange(S, dtype=np.float64)
    freqs = tt[:, None] * inv_freq[None, :]
    F = np.concatenate([freqs, freqs], axis=1)
    basev = (half + 0.4 * DH) / (1.4 * DH)
    power = (tt - S // 2) / float(WIN // 2)
    sc = basev[None, :] ** power[:, None]
    SC = np.concatenate([sc, sc], axis=1)
    sgn = np.where(np.arange(DH) < DH // 2, -1.0, 1.0)[:, None]
    Cq = (QKS * np.cos(F) * SC).T
    Sq = QKS * sgn * (np.sin(F) * SC).T
    Ck = (np.cos(F) / SC).T
    Sk = sgn * (np.sin(F) / SC).T
    for name, a in (("rotCq", Cq), ("rotSq", Sq), ("rotCk", Ck), ("rotSk", Sk)):
        t[name] = np.ascontiguousarray(np.concatenate([a, a], axis=0), dtype=np.float32)
    t["ident"] = np.eye(128, dtype=np.float32)
    hdsel = np.zeros((128, 4, 8), np.float32)
    for p in range(128):
        for c in range(4):
            hdsel[p, c, 2 * c + (1 if p >= 64 else 0)] = 1.0
    t["hdsel"] = hdsel
    bcsel = np.zeros((8, 4, 128), np.float32)
    for r in range(8):
        for c in range(4):
            for p in range(128):
                if r == 2 * c + (1 if p >= 64 else 0):
                    bcsel[r, c, p] = 1.0
    t["bcsel"] = bcsel
    return t


def host_weights(inp):
    w = {}
    permA = np.empty(1024, np.int64)
    for m in range(8):
        for p in range(128):
            head = m if p < 64 else m + 8
            permA[m * 128 + p] = head * 64 + (p % 64)
    w["A_wq"] = np.ascontiguousarray(inp["L0_attn_Wq"][:, permA])
    w["A_wk"] = np.ascontiguousarray(inp["L0_attn_Wk"])
    w["A_wv"] = np.ascontiguousarray(inp["L0_attn_Wv"])
    w["A_wo"] = np.ascontiguousarray(inp["L0_attn_Wo"][permA, :])
    _pc = lambda b: np.ascontiguousarray(np.asarray(b, np.float32).reshape(-1, 128).T)
    w["A_bq"], w["A_bk"], w["A_bv"], w["A_bo"] = (
        _pc(inp["L0_attn_bq"][permA]), _pc(inp["L0_attn_bk"]), _pc(inp["L0_attn_bv"]),
        _pc(inp["L0_attn_bo"]))
    HID, HP = 2730, 2816
    permR = np.empty(1024, np.int64)
    for f in range(1024):
        h, d = f // 64, f % 64
        permR[f] = h * 64 + (d + 32) % 64
    for l in range(2):
        w[f"B{l}_wqkv"] = np.ascontiguousarray(inp["L0_ffn_wqkv"][l])
        w[f"B{l}_wqkvp"] = np.ascontiguousarray(inp["L0_ffn_wqkv"][l][:, :1024][:, permR])
        w[f"B{l}_wo"] = np.ascontiguousarray(inp["L0_ffn_wo"][l])
        qs = np.asarray(inp["L0_ffn_q_scale"][l], np.float32)
        ks = np.asarray(inp["L0_ffn_k_scale"][l], np.float32)
        qsp = np.concatenate([qs[32:], qs[:32]])
        ksp = np.concatenate([ks[32:], ks[:32]])
        w[f"B{l}_qs4"] = np.ascontiguousarray(
            np.stack([np.tile(qs, 2), np.tile(qsp, 2),
                      np.tile(ks, 2), np.tile(ksp, 2)], axis=1))
        w[f"B{l}_lg"] = _pc(inp["L0_ffn_ln_g"][l])
        w[f"B{l}_lb"] = _pc(inp["L0_ffn_ln_b"][l])
        w1, b1 = inp["L0_ffn_w1"][l], inp["L0_ffn_b1"][l]
        w1p = np.zeros((D, 2 * HP), np.float32)
        w1p[:, :HID] = w1[:, :HID]
        w1p[:, HP:HP + HID] = w1[:, HID:]
        b1p = np.zeros(2 * HP, np.float32)
        b1p[:HID] = b1[:HID]
        b1p[HP:HP + HID] = b1[HID:]
        w2p = np.zeros((HP, D), np.float32)
        w2p[:HID] = inp["L0_ffn_w2"][l]
        K1, M1 = w1p.shape
        w[f"B{l}_w1"] = np.ascontiguousarray(w1p.reshape(K1, M1 // 128, 128).transpose(1, 0, 2))
        K2, M2 = w2p.shape
        w[f"B{l}_w2"] = np.ascontiguousarray(w2p.reshape(K2, M2 // 128, 128).transpose(1, 0, 2))
        w[f"B{l}_b1"] = _pc(b1p)
        w[f"B{l}_b2"] = _pc(inp["L0_ffn_b2"][l])
    def _groups(W):
        K, M = W.shape
        return np.ascontiguousarray(W.reshape(K, M // 128, 128).transpose(1, 0, 2))

    def _pcol(b):
        return np.ascontiguousarray(b.reshape(-1, 128).T)
    w["D_inproj"] = _groups(inp["L1_attn_in_proj_w"].T)
    cw = inp["L1_attn_conv_w"]
    G, CG = 8, 256
    convT = np.zeros((4, G, CG, CG), np.float32)
    for g in range(G):
        convT[:, g] = np.transpose(cw[g * CG:(g + 1) * CG], (2, 1, 0))
    w["D_conv"] = np.ascontiguousarray(convT)
    w["D_convb"] = _pcol(inp["L1_attn_conv_b"])
    w["D_lg"], w["D_lb"] = _pcol(inp["L1_attn_ln_g"]), _pcol(inp["L1_attn_ln_b"])
    w["D_fc"] = np.ascontiguousarray(inp["L1_attn_fc_w"].T)
    w["D_fcb"] = _pcol(inp["L1_attn_fc_b"])
    w["D_cproj"] = _groups(inp["L1_attn_cproj_w"].T)
    w["D_cprojb"] = _pcol(inp["L1_attn_cproj_b"])
    w["D_outw"] = _groups(inp["L1_attn_out_w"].T)
    w["E_wg"] = _groups(inp["L1_ffn_w_gate"].T.astype(np.int32))
    w["E_wu"] = _groups(inp["L1_ffn_w_up"].T.astype(np.int32))
    w["E_wd"] = _groups(inp["L1_ffn_w_down"].T.astype(np.int32))
    return w



class Sem:
    __slots__ = ("h", "val")

    def __init__(self, h):
        self.h = h
        self.val = 0

    def inc(self, n=1):
        self.val += n
        return self.val


class Bank:
    __slots__ = ("t", "guard")

    def __init__(self, t):
        self.t = t
        self.guard = None

    def ap(self):
        return self.t


class Prog:
    def __init__(self, nc):
        self.nc = nc
        self.q = {e: [] for e in ENGINES}
        self._stack = contextlib.ExitStack()
        self.sems = {}
        total = 224 * 1024
        self.soff = (total - nc.sbuf_bytes_remaining + 63) & ~63
        self.limit = total
        self.uid = 0

    def sem(self, name):
        if name not in self.sems:
            h = self._stack.enter_context(self.nc.semaphore(name))
            self.sems[name] = Sem(h)
        return self.sems[name]

    def emit(self, eng, fn):
        self.q[eng].append(fn)

    def wait(self, eng, sem, val):
        if val > 0:
            self.emit(eng, lambda e, s=sem.h, v=val: e.wait_ge(s, v))

    def inc(self, eng, sem, n=1):
        if self.q[eng]:
            last = self.q[eng].pop()
            self.q[eng].append(lambda e, f=last, s=sem.h, n=n: f(e).then_inc(s, n))
        else:
            self.emit(eng, lambda e, s=sem.h, n=n: e.nop().then_inc(s, n))
        return sem.inc(n)

    def barrier(self):
        b = self.sem("bar")
        for e in ENGINES:
            self.emit(e, lambda en, s=b.h: en.nop().then_inc(s, 1))
        b.val += len(ENGINES)
        v = b.val
        for e in ENGINES:
            self.wait(e, b, v)

    def dma(self, eng, out, in_, sem):
        self.emit(eng, lambda e, o=out, i=in_, s=sem.h: e.dma_start(out=o, in_=i).then_inc(s, 16))
        return sem.inc(16)

    def nbytes(self, shape, dtype):
        per = int(np.prod(shape[1:])) * ESIZE[dtype]
        return (per + 63) & ~63

    def galloc(self, name, shape, dtype):
        t = self.nc.alloc_sbuf_tensor_at(name, list(shape), dtype, offset=self.soff)
        self.soff += self.nbytes(shape, dtype)
        assert self.soff <= self.limit, f"SBUF global overflow at {name}: {self.soff}"
        return t

    def stage_begin(self):
        self.stage_base = self.soff

    def stage_end(self):
        self.soff = self.stage_base

    def salloc(self, name, shape, dtype):
        self.uid += 1
        t = self.nc.alloc_sbuf_tensor_at(f"{name}_{self.uid}", list(shape), dtype,
                                         offset=self.soff)
        self.soff += self.nbytes(shape, dtype)
        assert self.soff <= self.limit, f"SBUF stage overflow at {name}: {self.soff // 1024}KB"
        return t

    def make_banks(self):
        self.banks = [Bank(self.nc.alloc_psum_tensor(f"pb{i}", [128, 512], F32))
                      for i in range(5)]
        self.bank_s = Bank(self.nc.alloc_psum_tensor("pbsS", [1, 512], F32))
        self.bank_q = Bank(self.nc.alloc_psum_tensor("pbsQ", [1, 512], F32))
        self.bank_t = Bank(self.nc.alloc_psum_tensor("pbt", [128, 1024], BF16))

    def mm_group(self, bank, mms, out_guard_sem=None):
        mm = self.sem("mm")
        if bank.guard is not None:
            gsem, gval = bank.guard
            self.wait("tensor", gsem, gval)
            bank.guard = None
        n = len(mms)
        for idx, (o, l, r) in enumerate(mms):
            self.emit("tensor", lambda e, o=o, l=l, r=r, s=(idx == 0), p=(idx == n - 1):
                      e.matmul(o, l, r, start=s, stop=p))
        return self.inc("tensor", mm)

    def consume(self, bank, eng, sem_val=None):
        sem = self.engine_sem(eng)
        bank.guard = (sem, sem.val)

    def engine_sem(self, eng):
        return self.sem({"vector": "ev", "scalar": "av", "gpsimd": "pv",
                         "tensor": "mm", "sync": "sv"}[eng])

    def finalize(self):
        with self.nc.Block() as block:
            for e in ENGINES:
                getattr(block, e)(lambda eng, e=e: [fn(eng) for fn in self.q[e]])



class Globals:
    pass


def setup_globals(P, S, td):
    nc = P.nc
    g = Globals()
    g.S = S
    g.NT = S // TT
    g.dram = {}
    for name, arr in td.items():
        g.dram[name] = nc.dram_tensor(name, list(arr.shape), F32, kind="ExternalInput").ap()
    g.ident = P.galloc("g_ident", [128, 128], BF16)
    g.maskA = P.galloc("g_maskA", [128, 512], F32)
    g.maskB = P.galloc("g_maskB", [128, 5, 640], F32)
    g.ones10 = P.galloc("g_ones10", [128, 2], BF16)
    g.ones01 = P.galloc("g_ones01", [128, 2], BF16)
    g.ones1r = P.galloc("g_ones1r", [1, 128], BF16)
    g.hdsel = P.galloc("g_hdsel", [128, 4, 8], BF16)
    g.bcsel = P.galloc("g_bcsel", [8, 4, 128], BF16)
    g.stats = P.galloc("g_stats", [1, 4, TT], F32)
    g.statsb = P.galloc("g_statsb", [1, 2, TT], BF16)
    g.x2 = P.galloc("g_x2", [128, 16, TT], BF16)
    g.lnt = P.galloc("g_lnt", [128, TT], F32)
    ld = P.sem("gld")
    P.dma("sync", g.maskA[:, :], g.dram["maskA"][:, :], ld)
    for m in range(5):
        P.dma("sync", g.maskB[:, m, :], g.dram[f"maskB{m}"][:, :], ld)
    P.emit("gpsimd", lambda e: e.dma_start(out=g.ident[:, :], in_=g.dram["ident"][:, :]).then_inc(ld.h, 16))
    ld.inc(16)
    P.emit("gpsimd", lambda e: e.dma_start(out=g.hdsel[:, :, :], in_=g.dram["hdsel"][:, :, :]).then_inc(ld.h, 16))
    ld.inc(16)
    P.emit("gpsimd", lambda e: e.dma_start(out=g.bcsel[:, :, :], in_=g.dram["bcsel"][:, :, :]).then_inc(ld.h, 16))
    ld.inc(16)
    P.emit("gpsimd", lambda e: e.memset(g.ones10[:, :], 0.0))
    P.emit("gpsimd", lambda e: e.memset(g.ones01[:, :], 0.0))
    P.emit("gpsimd", lambda e: e.memset(g.ones10[:, 0:1], 1.0))
    P.emit("gpsimd", lambda e: e.memset(g.ones01[:, 1:2], 1.0))
    P.emit("gpsimd", lambda e: e.memset(g.ones1r[:, :], 1.0))
    gv = P.inc("gpsimd", P.sem("pv"))
    for e in ENGINES:
        P.wait(e, ld, ld.val)
        P.wait(e, P.sem("pv"), gv)
    return g



class XTiles:
    _uid = [0]

    def __init__(self, P, g, src, nchunks=8, want_f32=True, want_bf16=True, nbuf=2,
                 src16=None):
        self.P, self.g, self.src, self.src16 = P, g, src, src16
        self.n = nchunks
        self.nbuf = nbuf
        self.want_f32, self.want_bf16 = want_f32, want_bf16
        W = HALO + TT
        self.f32 = ([P.salloc("xt32", [128, nchunks, W], F32) for _ in range(nbuf)]
                    if want_f32 else None)
        self.b16 = ([P.salloc("xt16", [128, nchunks, W], BF16) for _ in range(nbuf)]
                    if want_bf16 else None)
        XTiles._uid[0] += 1
        u = XTiles._uid[0]
        self.sems = [P.sem(f"xt{u}_{p}") for p in range(nbuf)]
        self.ready = [0] * nbuf
        self.users = [None] * nbuf
        self.halo_done = 0

    def load(self, i):
        P = self.P
        p = i % self.nbuf
        if self.users[p] is not None:
            s, v = self.users[p]
            P.wait("sync", s, v)
            P.wait("gpsimd", s, v)
            self.users[p] = None
        t0 = i * TT
        lo = t0 - HALO
        sem = self.sems[p]
        s16 = self.src16 if self.src16 is not None else self.src
        if lo < 0:
            if self.want_f32:
                P.emit("gpsimd", lambda e, p=p: e.memset(self.f32[p][:, :, 0:HALO], 0.0))
            if self.want_bf16:
                P.emit("gpsimd", lambda e, p=p: e.memset(self.b16[p][:, :, 0:HALO], 0.0))
            self.halo_done = P.inc("gpsimd", P.sem("pv"))
            for c in range(self.n):
                if self.want_f32:
                    P.dma("sync", self.f32[p][:, c, HALO:], self.src[c * 128:(c + 1) * 128, 0:TT], sem)
                if self.want_bf16:
                    P.dma("gpsimd", self.b16[p][:, c, HALO:], s16[c * 128:(c + 1) * 128, 0:TT], sem)
        else:
            for c in range(self.n):
                if self.want_f32:
                    P.dma("sync", self.f32[p][:, c, :], self.src[c * 128:(c + 1) * 128, lo:t0 + TT], sem)
                if self.want_bf16:
                    P.dma("gpsimd", self.b16[p][:, c, :], s16[c * 128:(c + 1) * 128, lo:t0 + TT], sem)
        self.ready[p] = sem.val

    def wait_tile(self, eng, i):
        p = i % self.nbuf
        self.P.wait(eng, self.sems[p], self.ready[p])
        if i == 0 and self.halo_done:
            self.P.wait(eng, self.P.sem("pv"), self.halo_done)

    def release(self, i, sem, val):
        self.users[i % self.nbuf] = (sem, val)


class WStream:
    _uid = [0]

    def __init__(self, P, dram, ngroups, kch, nslots=4, src_is_bf16=False):
        self.P, self.dram = P, dram
        self.ngroups, self.kch, self.nslots = ngroups, kch, nslots
        WStream._uid[0] += 1
        u = WStream._uid[0]
        self.slots = [P.salloc(f"ws{u}", [128, kch, 128], BF16) for _ in range(nslots)]
        self.sems = [P.sem(f"ws{u}_{s}") for s in range(nslots)]
        self.ready = {}
        self.user = [None] * nslots
        self.loaded = -1

    def load(self, gi):
        if gi >= self.ngroups or gi <= self.loaded:
            return
        self.loaded = gi
        P = self.P
        s = gi % self.nslots
        if self.user[s] is not None:
            sem, v = self.user[s]
            P.wait("gpsimd", sem, v)
            self.user[s] = None
        srcg = self.dram[gi].rearrange("(c p) n -> p c n", p=128)
        P.dma("gpsimd", self.slots[s][:, :, :], srcg, self.sems[s])
        self.ready[gi] = self.sems[s].val

    def use(self, gi):
        s = gi % self.nslots
        self.P.wait("tensor", self.sems[s], self.ready[gi])
        return self.slots[s]

    def done(self, gi):
        mm = self.P.sem("mm")
        self.user[gi % self.nslots] = (mm, mm.val)



def emit_layernorm(P, g, xf32_chunks, xb16_chunks, ncols, xhat, nchunks=8,
                   affine=None, eps=1e-5):
    nc = P.nc
    NF = float(nchunks * 128)
    mm, ev, av = P.sem("mm"), P.sem("ev"), P.sem("av")
    bs, bq = P.bank_s, P.bank_q
    b1, b2 = P.banks[3], P.banks[4]
    for c in range(nchunks):
        P.emit("scalar", lambda e, c=c: e.activation(
            out=g.x2[:, c, 0:ncols], in_=xf32_chunks(c), func=AF.Square, bias=0.0, scale=1.0))
    sq_done = P.inc("scalar", av)
    for b in (bs, bq):
        if b.guard is not None:
            P.wait("tensor", b.guard[0], b.guard[1])
            b.guard = None
    P.wait("tensor", av, sq_done)
    for c in range(nchunks):
        P.emit("tensor", lambda e, c=c, n=nchunks: e.matmul(
            bs.t[0:1, 0:ncols], g.ones10[:, 0:1], xb16_chunks(c),
            start=(c == 0), stop=(c == n - 1)))
    for c in range(nchunks):
        P.emit("tensor", lambda e, c=c, n=nchunks: e.matmul(
            bq.t[0:1, 0:ncols], g.ones10[:, 0:1], g.x2[:, c, 0:ncols],
            start=(c == 0), stop=(c == n - 1)))
    stats_done = P.inc("tensor", mm)

    st = g.stats
    P.wait("vector", mm, stats_done)
    P.emit("vector", lambda e: e.tensor_copy(out=st[0:1, 0, 0:ncols], in_=bs.t[0:1, 0:ncols]))
    P.emit("vector", lambda e: e.tensor_tensor(
        out=st[0:1, 1, 0:ncols], in0=st[0:1, 0, 0:ncols], in1=st[0:1, 0, 0:ncols], op=ALU.mult))
    P.emit("vector", lambda e: e.scalar_tensor_tensor(
        out=st[0:1, 1, 0:ncols], in0=bq.t[0:1, 0:ncols], scalar=NF,
        in1=st[0:1, 1, 0:ncols], op0=ALU.mult, op1=ALU.subtract))
    P.emit("vector", lambda e: e.tensor_scalar(
        out=st[0:1, 1, 0:ncols], in0=st[0:1, 1, 0:ncols], scalar1=1.0 / (NF * NF),
        scalar2=eps, op0=ALU.mult, op1=ALU.add))
    var_done = P.inc("vector", ev)
    bs.guard = (ev, var_done)
    bq.guard = (ev, var_done)
    P.wait("scalar", ev, var_done)
    P.emit("scalar", lambda e: e.activation(out=st[0:1, 1, 0:ncols], in_=st[0:1, 1, 0:ncols],
                                            func=AF.Ln, bias=0.0, scale=1.0))
    P.emit("scalar", lambda e: e.activation(out=st[0:1, 2, 0:ncols], in_=st[0:1, 1, 0:ncols],
                                            func=AF.Exp, bias=0.0, scale=-0.5))
    rstd_done = P.inc("scalar", av)
    P.wait("vector", av, rstd_done)
    P.emit("vector", lambda e: e.scalar_tensor_tensor(
        out=st[0:1, 3, 0:ncols], in0=st[0:1, 0, 0:ncols], scalar=-1.0 / NF,
        in1=st[0:1, 2, 0:ncols], op0=ALU.mult, op1=ALU.mult))
    P.emit("vector", lambda e: e.tensor_copy(out=g.statsb[0:1, 0, 0:ncols], in_=st[0:1, 2, 0:ncols]))
    P.emit("vector", lambda e: e.tensor_copy(out=g.statsb[0:1, 1, 0:ncols], in_=st[0:1, 3, 0:ncols]))
    rows_done = P.inc("vector", ev)

    for b in (b1, b2):
        if b.guard is not None:
            P.wait("tensor", b.guard[0], b.guard[1])
            b.guard = None
    P.wait("tensor", ev, rows_done)
    P.emit("tensor", lambda e: e.matmul(b1.t[:, 0:ncols], g.ones1r[0:1, :],
                                        g.statsb[0:1, 0, 0:ncols], start=True, stop=True))
    P.emit("tensor", lambda e: e.matmul(b2.t[:, 0:ncols], g.ones1r[0:1, :],
                                        g.statsb[0:1, 1, 0:ncols], start=True, stop=True))
    bcast_done = P.inc("tensor", mm)

    P.wait("vector", mm, bcast_done)
    for c in range(nchunks):
        P.emit("vector", lambda e, c=c: e.tensor_tensor(
            out=g.lnt[:, 0:ncols], in0=xf32_chunks(c), in1=b1.t[:, 0:ncols], op=ALU.mult))
        if affine is None:
            P.emit("vector", lambda e, c=c: e.tensor_tensor(
                out=xhat[:, c, 0:ncols], in0=g.lnt[:, 0:ncols], in1=b2.t[:, 0:ncols], op=ALU.add))
        else:
            gg, bb = affine
            P.emit("vector", lambda e, c=c: e.tensor_tensor(
                out=g.lnt[:, 0:ncols], in0=g.lnt[:, 0:ncols], in1=b2.t[:, 0:ncols], op=ALU.add))
            P.emit("vector", lambda e, c=c: e.tensor_scalar(
                out=xhat[:, c, 0:ncols], in0=g.lnt[:, 0:ncols],
                scalar1=gg[:, c:c + 1], scalar2=bb[:, c:c + 1], op0=ALU.mult, op1=ALU.add))
    done = P.inc("vector", ev)
    b1.guard = (ev, done)
    b2.guard = (ev, done)
    return done



def emit_stage_E(P, g, src, dst, wd):
    S, NT = g.S, g.NT
    P.stage_begin()
    xt = XTiles(P, g, src, want_f32=True, want_bf16=True, nbuf=2)
    xhat = P.salloc("e_xhat", [128, 8, TT], BF16)
    silu_g = P.salloc("e_silug", [128, 24, TT], BF16)
    hbuf = silu_g
    xnst = P.salloc("e_xnst", [128, 2, TT], F32)
    ws_g = WStream(P, wd["E_wg"], 24, 8)
    ws_u = WStream(P, wd["E_wu"], 24, 8)
    ws_d = WStream(P, wd["E_wd"], 8, 24, nslots=3)
    ev, av, mm = P.sem("ev"), P.sem("av"), P.sem("mm")
    ssem = P.sem("est")

    for i in range(NT):
        xt.load(i)
        for k in range(3):
            ws_g.load(k)
        p = i % xt.nbuf
        xf = lambda c, p=p: xt.f32[p][:, c, HALO:HALO + TT]
        xb = lambda c, p=p: xt.b16[p][:, c, HALO:HALO + TT]
        for e in ("scalar", "tensor", "vector"):
            xt.wait_tile(e, i)
        ln_done = emit_layernorm(P, g, xf, xb, TT, xhat)
        P.wait("tensor", ev, ln_done)
        for m in range(24):
            ws_g.load(m + 3)
            w = ws_g.use(m)
            bank = P.banks[m % 3]
            mmv = P.mm_group(bank, [(bank.t[:, 0:TT], w[:, c, :], xhat[:, c, :])
                                    for c in range(8)])
            ws_g.done(m)
            P.wait("scalar", mm, mmv)
            P.emit("scalar", lambda e, m=m, bank=bank: e.activation(
                out=silu_g[:, m, :], in_=bank.t[:, 0:TT], func=AF.Silu,
                bias=0.0, scale=1.0 / 127.0))
            bank.guard = (av, P.inc("scalar", av))
        gate_done = av.val
        for k in range(3):
            ws_u.load(k)
        P.wait("vector", av, gate_done)
        for m in range(24):
            ws_u.load(m + 3)
            w = ws_u.use(m)
            bank = P.banks[m % 3]
            mmv = P.mm_group(bank, [(bank.t[:, 0:TT], w[:, c, :], xhat[:, c, :])
                                    for c in range(8)])
            ws_u.done(m)
            P.wait("vector", mm, mmv)
            P.emit("vector", lambda e, m=m, bank=bank: e.scalar_tensor_tensor(
                out=hbuf[:, m, :], in0=bank.t[:, 0:TT], scalar=1.0 / 127.0,
                in1=silu_g[:, m, :], op0=ALU.mult, op1=ALU.mult))
            bank.guard = (ev, P.inc("vector", ev))
        h_done = ev.val
        for k in range(2):
            ws_d.load(k)
        P.wait("tensor", ev, h_done)
        st_hist = []
        for m in range(8):
            ws_d.load(m + 2)
            w = ws_d.use(m)
            bank = P.banks[m % 3]
            mmv = P.mm_group(bank, [(bank.t[:, 0:TT], w[:, c, :], hbuf[:, c, :])
                                    for c in range(24)])
            ws_d.done(m)
            P.wait("vector", mm, mmv)
            if len(st_hist) >= 2:
                P.wait("vector", ssem, st_hist[-2])
            P.emit("vector", lambda e, m=m, bank=bank, p=p: e.scalar_tensor_tensor(
                out=xnst[:, m % 2, :], in0=bank.t[:, 0:TT], scalar=1.0 / 127.0,
                in1=xt.f32[p][:, m, HALO:HALO + TT], op0=ALU.mult, op1=ALU.add))
            bank.guard = (ev, P.inc("vector", ev))
            P.wait("sync", ev, ev.val)
            P.dma("sync", dst[m * 128:(m + 1) * 128, i * TT:(i + 1) * TT],
                  xnst[:, m % 2, :], ssem)
            st_hist.append(ssem.val)
        out_ready = ev.val
        xt.release(i, ev, out_ready)
        ws_g.loaded = -1
        ws_u.loaded = -1
        ws_d.loaded = -1
    P.wait("sync", ssem, ssem.val)
    P.stage_end()



def emit_stage_A(P, g, src, dst, wd):
    S, NT = g.S, g.NT
    P.stage_begin()
    xt = XTiles(P, g, src, want_f32=True, want_bf16=True, nbuf=2)
    xhat = P.salloc("a_xhat", [128, 8, TT], BF16)
    q_sb = [P.salloc("a_q", [128, 8, TT], BF16) for _ in range(2)]
    kring = P.salloc("a_kring", [128, 4, TT], BF16)
    vstage = P.salloc("a_vst", [128, TT], BF16)
    vtring = P.salloc("a_vtr", [128, 16, 128], BF16)
    sc32 = P.salloc("a_sc", [128, 2, TT], F32)
    pexp = P.salloc("a_pexp", [128, 2, TT], BF16)
    small = P.salloc("a_small", [128, 8], F32)
    ptr = P.salloc("a_ptr", [128, 2, 4, 128], BF16)
    o_sb = P.salloc("a_o", [128, 8, TT], BF16)
    xnst = P.salloc("a_xnst", [128, 2, TT], F32)
    wq = P.salloc("a_wq", [128, 8, 1024], BF16)
    wk = P.salloc("a_wk", [128, 8, 128], BF16)
    wv = P.salloc("a_wv", [128, 8, 128], BF16)
    wo = P.salloc("a_wo", [128, 8, 1024], BF16)
    bq = P.salloc("a_bq", [128, 8], F32)
    bk = P.salloc("a_bk", [128, 1], F32)
    bv = P.salloc("a_bv", [128, 1], F32)
    bo = P.salloc("a_bo", [128, 8], F32)
    ev, av, mm, pv = P.sem("ev"), P.sem("av"), P.sem("mm"), P.sem("pv")
    ssem = P.sem("ast")
    wld = P.sem("awld")
    for c in range(8):
        P.dma("gpsimd", wq[:, c, :], wd["A_wq"][c * 128:(c + 1) * 128, :], wld)
        P.dma("gpsimd", wk[:, c, :], wd["A_wk"][c * 128:(c + 1) * 128, :], wld)
        P.dma("gpsimd", wv[:, c, :], wd["A_wv"][c * 128:(c + 1) * 128, :], wld)
        P.dma("gpsimd", wo[:, c, :], wd["A_wo"][c * 128:(c + 1) * 128, :], wld)
    P.dma("sync", bq[:, :], wd["A_bq"][:, :], wld)
    P.dma("sync", bk[:, :], wd["A_bk"][:, :], wld)
    P.dma("sync", bv[:, :], wd["A_bv"][:, :], wld)
    P.dma("sync", bo[:, :], wd["A_bo"][:, :], wld)
    P.emit("gpsimd", lambda e: e.memset(kring[:, 3, :], 0.0))
    P.emit("gpsimd", lambda e: e.memset(vtring[:, 12:16, :], 0.0))
    pad0_done = P.inc("gpsimd", pv)
    for e in ("tensor", "vector", "scalar"):
        P.wait(e, wld, wld.val)
        P.wait(e, pv, pad0_done)

    k_ready = {}

    def emit_qkv(i):
        p = i % xt.nbuf
        xf = lambda c, p=p: xt.f32[p][:, c, HALO:HALO + TT]
        xb = lambda c, p=p: xt.b16[p][:, c, HALO:HALO + TT]
        for e in ("scalar", "tensor", "vector"):
            xt.wait_tile(e, i)
        ln_done = emit_layernorm(P, g, xf, xb, TT, xhat)
        P.wait("tensor", ev, ln_done)
        qp = i % 2
        for m in range(8):
            bank = P.banks[m % 3]
            mmv = P.mm_group(bank, [(bank.t[:, 0:TT], wq[:, c, m * 128:(m + 1) * 128],
                                     xhat[:, c, :]) for c in range(8)])
            if m % 2 == 0:
                P.wait("vector", mm, mmv)
                P.emit("vector", lambda e, m=m, bank=bank, qp=qp: e.tensor_scalar(
                    out=q_sb[qp][:, m, :], in0=bank.t[:, 0:TT], scalar1=bq[:, m:m + 1],
                    scalar2=None, op0=ALU.add))
                bank.guard = (ev, P.inc("vector", ev))
            else:
                P.wait("scalar", mm, mmv)
                P.emit("scalar", lambda e, m=m, bank=bank, qp=qp: e.add(
                    out=q_sb[qp][:, m, :], in_=bank.t[:, 0:TT], add=bq[:, m:m + 1]))
                bank.guard = (av, P.inc("scalar", av))
        bank = P.banks[0]
        mmv = P.mm_group(bank, [(bank.t[:, 0:TT], wk[:, c, :], xhat[:, c, :])
                                for c in range(8)])
        P.wait("vector", mm, mmv)
        P.emit("vector", lambda e, bank=bank, i=i: e.tensor_scalar(
            out=kring[:, i % 4, :], in0=bank.t[:, 0:TT], scalar1=bk[:, 0:1],
            scalar2=None, op0=ALU.add))
        bank.guard = (ev, P.inc("vector", ev))
        bank = P.banks[1]
        mmv = P.mm_group(bank, [(bank.t[:, 0:TT], wv[:, c, :], xhat[:, c, :])
                                for c in range(8)])
        P.wait("vector", mm, mmv)
        P.emit("vector", lambda e, bank=bank: e.tensor_scalar(
            out=vstage[:, :], in0=bank.t[:, 0:TT], scalar1=bv[:, 0:1],
            scalar2=None, op0=ALU.add))
        vst_done = P.inc("vector", ev)
        bank.guard = (ev, vst_done)
        bt = P.bank_t
        if bt.guard is not None:
            P.wait("tensor", bt.guard[0], bt.guard[1])
            bt.guard = None
        P.wait("tensor", ev, vst_done)
        for b in range(4):
            P.emit("tensor", lambda e, b=b: e.transpose(
                bt.t[:, b * 128:(b + 1) * 128], vstage[:, b * 128:(b + 1) * 128],
                g.ident[:, :]))
        tv = P.inc("tensor", mm)
        P.wait("vector", mm, tv)
        for b in range(4):
            P.emit("vector", lambda e, b=b, i=i: e.tensor_copy(
                out=vtring[:, (4 * i + b) % 16, :], in_=bt.t[:, b * 128:(b + 1) * 128]))
        vt_done = P.inc("vector", ev)
        bt.guard = (ev, vt_done)
        k_ready[i] = ev.val

    def score_pieces(j, s):
        if s == 0:
            return [((j - 1) % 4, 384, 512), (j % 4, 0, 384)]
        if s == 1:
            return [(j % 4, 0, 512)]
        if s == 2:
            return [(j % 4, 128, 512), ((j + 1) % 4, 0, 128)]
        return [(j % 4, 256, 512), ((j + 1) % 4, 0, 256)]

    def emit_attn(j):
        p = j % xt.nbuf
        qp = j % 2
        P.wait("tensor", ev, k_ready.get(j + 1, k_ready[j]))
        prev_exp = [0, 0]
        prev_norm = [0, 0]
        prev_ptr = [0, 0]
        for s in range(4):
            for h in range(16):
                unit = s * 16 + h
                par = unit % 2
                kv = h // 8
                base = 64 * kv
                qb = base
                bank = P.banks[par]
                mms = []
                off = 0
                for (slot, c0, c1) in score_pieces(j, s):
                    mms.append((bank.t[:, off:off + (c1 - c0)],
                                q_sb[qp][qb:qb + 64, h % 8, s * 128:(s + 1) * 128],
                                kring[base:base + 64, slot, c0:c1]))
                    off += c1 - c0
                if bank.guard is not None:
                    P.wait("tensor", bank.guard[0], bank.guard[1])
                    bank.guard = None
                for (o, l, r) in mms:
                    P.emit("tensor", lambda e, o=o, l=l, r=r: e.matmul(
                        o, l, r, start=True, stop=True))
                sc_done = P.inc("tensor", mm)
                P.wait("vector", mm, sc_done)
                P.wait("vector", av, prev_exp[par])
                P.emit("vector", lambda e, bank=bank, par=par: e.tensor_tensor(
                    out=sc32[:, par, :], in0=bank.t[:, 0:TT], in1=g.maskA[:, :], op=ALU.add))
                bank.guard = (ev, P.inc("vector", ev))
                P.emit("vector", lambda e, par=par: e.reduce_max(
                    out=small[:, 4 * par:4 * par + 1], in_=sc32[:, par, :], axis=AX.X))
                P.emit("vector", lambda e, par=par: e.tensor_scalar(
                    out=small[:, 4 * par + 1:4 * par + 2], in0=small[:, 4 * par:4 * par + 1],
                    scalar1=-1.0, scalar2=None, op0=ALU.mult))
                mx_done = P.inc("vector", ev)
                P.wait("scalar", ev, mx_done)
                P.wait("scalar", mm, prev_ptr[par])
                P.emit("scalar", lambda e, par=par: e.activation(
                    out=pexp[:, par, :], in_=sc32[:, par, :], func=AF.Exp,
                    bias=small[:, 4 * par + 1:4 * par + 2], scale=1.0,
                    accum_out=small[:, 4 * par + 2:4 * par + 3]))
                prev_exp[par] = P.inc("scalar", av)
                P.wait("vector", av, prev_exp[par])
                P.emit("vector", lambda e, par=par: e.reciprocal(
                    out=small[:, 4 * par + 3:4 * par + 4],
                    in_=small[:, 4 * par + 2:4 * par + 3]))
                rc_done = P.inc("vector", ev)
                P.wait("scalar", ev, rc_done)
                P.emit("scalar", lambda e, par=par: e.mul(
                    out=pexp[:, par, :], in_=pexp[:, par, :],
                    mul=small[:, 4 * par + 3:4 * par + 4]))
                norm_done = P.inc("scalar", av)
                prev_norm[par] = norm_done
                bt = P.bank_t
                if bt.guard is not None:
                    P.wait("tensor", bt.guard[0], bt.guard[1])
                    bt.guard = None
                P.wait("tensor", av, norm_done)
                for u in range(4):
                    P.emit("tensor", lambda e, u=u, par=par: e.transpose(
                        bt.t[:, par * 512 + u * 128:par * 512 + (u + 1) * 128],
                        pexp[:, par, u * 128:(u + 1) * 128], g.ident[:, :]))
                prev_ptr[par] = P.inc("tensor", mm)
                P.wait("vector", mm, prev_ptr[par])
                for u in range(4):
                    P.emit("vector", lambda e, u=u, par=par: e.tensor_copy(
                        out=ptr[:, par, u, :],
                        in_=bt.t[:, par * 512 + u * 128:par * 512 + (u + 1) * 128]))
                ptr_done = P.inc("vector", ev)
                bt.guard = (ev, ptr_done)
                avb = P.banks[2 + par]
                if avb.guard is not None:
                    P.wait("tensor", avb.guard[0], avb.guard[1])
                    avb.guard = None
                P.wait("tensor", ev, ptr_done)
                for u in range(4):
                    P.emit("tensor", lambda e, u=u, j=j, s=s, base=base, par=par: e.matmul(
                        avb.t[0:64, 0:128], vtring[:, (4 * j + s - 1 + u) % 16, base:base + 64],
                        ptr[:, par, u, :], start=(u == 0), stop=(u == 3)))
                av_done = P.inc("tensor", mm)
                P.wait("vector", mm, av_done)
                P.emit("vector", lambda e, h=h, s=s: e.tensor_copy(
                    out=o_sb[64 * (h // 8):64 * (h // 8) + 64, h % 8, s * 128:(s + 1) * 128],
                    in_=avb.t[0:64, 0:128]))
                avb.guard = (ev, P.inc("vector", ev))
        o_done = ev.val
        P.wait("tensor", ev, o_done)
        st_hist = []
        for m in range(8):
            bank = P.banks[3 + m % 2]
            mmv = P.mm_group(bank, [(bank.t[:, 0:TT], wo[:, c, m * 128:(m + 1) * 128],
                                     o_sb[:, c, :]) for c in range(8)])
            P.wait("vector", mm, mmv)
            if len(st_hist) >= 2:
                P.wait("vector", ssem, st_hist[-2])
            P.emit("vector", lambda e, m=m, bank=bank, p=p: e.scalar_tensor_tensor(
                out=xnst[:, m % 2, :], in0=bank.t[:, 0:TT], scalar=bo[:, m:m + 1],
                in1=xt.f32[p][:, m, HALO:HALO + TT], op0=ALU.add, op1=ALU.add))
            bank.guard = (ev, P.inc("vector", ev))
            P.wait("sync", ev, ev.val)
            P.dma("sync", dst[m * 128:(m + 1) * 128, j * TT:(j + 1) * TT],
                  xnst[:, m % 2, :], ssem)
            st_hist.append(ssem.val)
        xt.release(j, ev, ev.val)

    xt.load(0)
    emit_qkv(0)
    for j in range(NT):
        if j + 1 < NT:
            xt.load(j + 1)
            emit_qkv(j + 1)
        else:
            P.wait("gpsimd", mm, mm.val)
            P.emit("gpsimd", lambda e: e.memset(kring[:, NT % 4, :], 0.0))
            P.emit("gpsimd", lambda e, a=(4 * NT) % 16: e.memset(vtring[:, a:a + 1, :], 0.0))
            P.emit("gpsimd", lambda e, a=(4 * NT + 1) % 16: e.memset(vtring[:, a:a + 1, :], 0.0))
            padN = P.inc("gpsimd", pv)
            P.wait("tensor", pv, padN)
            k_ready[NT] = k_ready[NT - 1]
        emit_attn(j)
    P.wait("sync", ssem, ssem.val)
    P.stage_end()



def emit_stage_LN(P, g, src, dst):
    S, NT = g.S, g.NT
    P.stage_begin()
    xt = XTiles(P, g, src, want_f32=True, want_bf16=True, nbuf=2)
    u32 = P.salloc("ln_u", [128, 8, TT], F32)
    ev = P.sem("ev")
    ssem = P.sem("lnst")
    store_done = 0
    for i in range(NT):
        xt.load(i)
        p = i % xt.nbuf
        xf = lambda c, p=p: xt.f32[p][:, c, HALO:HALO + TT]
        xb = lambda c, p=p: xt.b16[p][:, c, HALO:HALO + TT]
        for e in ("scalar", "tensor", "vector"):
            xt.wait_tile(e, i)
        P.wait("vector", ssem, store_done)
        ln_done = emit_layernorm(P, g, xf, xb, TT, u32)
        P.wait("sync", ev, ln_done)
        for c in range(8):
            P.dma("sync", dst[c * 128:(c + 1) * 128, i * TT:(i + 1) * TT],
                  u32[:, c, :], ssem)
        store_done = ssem.val
        xt.release(i, ev, ln_done)
    P.wait("sync", ssem, ssem.val)
    P.stage_end()


def emit_stage_ADD(P, g, src_a, src_b, dst):
    S, NT = g.S, g.NT
    P.stage_begin()
    ta = XTiles(P, g, src_a, want_f32=True, want_bf16=False, nbuf=2)
    tb = XTiles(P, g, src_b, want_f32=True, want_bf16=False, nbuf=2)
    o32 = P.salloc("add_o", [128, 8, TT], F32)
    ev = P.sem("ev")
    ssem = P.sem("addst")
    store_done = 0
    for i in range(NT):
        ta.load(i)
        tb.load(i)
        pa, pb = i % ta.nbuf, i % tb.nbuf
        ta.wait_tile("vector", i)
        tb.wait_tile("vector", i)
        P.wait("vector", ssem, store_done)
        for c in range(8):
            P.emit("vector", lambda e, c=c, pa=pa, pb=pb: e.tensor_add(
                out=o32[:, c, :], in0=ta.f32[pa][:, c, HALO:HALO + TT],
                in1=tb.f32[pb][:, c, HALO:HALO + TT]))
        done = P.inc("vector", ev)
        P.wait("sync", ev, done)
        for c in range(8):
            P.dma("sync", dst[c * 128:(c + 1) * 128, i * TT:(i + 1) * TT],
                  o32[:, c, :], ssem)
        store_done = ssem.val
        ta.release(i, ev, done)
        tb.release(i, ev, done)
    P.wait("sync", ssem, ssem.val)
    P.stage_end()



def emit_stage_B_attn(P, g, src, dst, wd, l):
    S, NT = g.S, g.NT
    P.stage_begin()
    xt = XTiles(P, g, src, want_f32=True, want_bf16=True, nbuf=2)
    q_sb = P.salloc("b_q", [128, 4, TT], BF16)
    k_sb = P.salloc("b_k", [128, 4, TT], BF16)
    qp_st = P.salloc("b_qp", [128, TT], BF16)
    s1 = P.salloc("b_s1", [128, TT], F32)
    s2 = P.salloc("b_s2", [128, TT], F32)
    rtab = P.salloc("b_rtab", [128, 4, TT], F32)
    qrot = P.salloc("b_qrot", [128, 4, TT], BF16)
    kring = P.salloc("b_kring", [128, 4, 3, TT], BF16)
    vtring = P.salloc("b_vtr", [128, 12, 4, 128], BF16)
    vstage = P.salloc("b_vst", [128, TT], BF16)
    sc32 = P.salloc("b_sc", [128, 2, 640], F32)
    pexp = P.salloc("b_pexp", [128, 2, 640], BF16)
    small = P.salloc("b_small", [128, 8], F32)
    rs16 = P.salloc("b_rs16", [8, 2, TT], BF16)
    ptr = P.salloc("b_ptr", [128, 2, 5, 128], BF16)
    o_sb = P.salloc("b_o", [128, 4, TT], BF16)
    xnst = P.salloc("b_xnst", [128, 2, TT], F32)
    wqkv = P.salloc("b_wqkv", [128, 8, 1536], BF16)
    wo = P.salloc("b_wo", [128, 4, 1024], BF16)
    qs2 = P.salloc("b_qs2", [128, 4], F32)
    hdsel, bcsel = g.hdsel, g.bcsel
    ev, av, mm, pv = P.sem("ev"), P.sem("av"), P.sem("mm"), P.sem("pv")
    ssem = P.sem("bst")
    wld = P.sem("bwld")
    tld = P.sem("btld")
    for c in range(8):
        P.dma("gpsimd", wqkv[:, c, :], wd[f"B{l}_wqkv"][c * 128:(c + 1) * 128, :], wld)
    for c in range(4):
        P.dma("gpsimd", wo[:, c, :], wd[f"B{l}_wo"][c * 128:(c + 1) * 128, :], wld)
    P.dma("sync", qs2[:, :], wd[f"B{l}_qs4"][:, :], wld)
    P.emit("gpsimd", lambda e: e.memset(kring[:, :, 2, :], 0.0))
    P.emit("gpsimd", lambda e: e.memset(vtring[:, 8:12, :, :], 0.0))
    pad0 = P.inc("gpsimd", pv)
    for e in ("tensor", "vector", "scalar"):
        P.wait(e, wld, wld.val)
        P.wait(e, pv, pad0)

    def shifted(buf, p, c):
        off = HALO if c < 4 else HALO - 1
        return buf[p][:, c, off:off + TT]

    rt_ready = 0

    def emit_qkv(i):
        nonlocal rt_ready
        p = i % xt.nbuf
        for e in ("tensor", "vector", "scalar"):
            xt.wait_tile(e, i)
        P.wait("sync", pv, rt_ready)
        for r, nm in enumerate(("rotCq", "rotSq", "rotCk", "rotSk")):
            P.dma("sync", rtab[:, r, :], g.dram[nm][:, i * TT:(i + 1) * TT], tld)
        tld_v = tld.val
        for part, dst_sb in ((0, q_sb), (1, k_sb)):
            for mc in range(4):
                m = part * 4 + mc
                bank = P.banks[m % 3]
                mmv = P.mm_group(bank, [(bank.t[:, 0:TT], wqkv[:, c, m * 128:(m + 1) * 128],
                                         shifted(xt.b16, p, c)) for c in range(8)])
                P.wait("vector", mm, mmv)
                P.emit("vector", lambda e, mc=mc, bank=bank, d=dst_sb: e.tensor_copy(
                    out=d[:, mc, :], in_=bank.t[:, 0:TT]))
                bank.guard = (ev, P.inc("vector", ev))
        qk_done = ev.val
        P.wait("gpsimd", ev, qk_done)
        for mc in range(4):
            P.emit("gpsimd", lambda e, mc=mc: e.tensor_mul(
                out=g.x2[:, mc, 0:TT], in0=q_sb[:, mc, :], in1=q_sb[:, mc, :]))
        for mc in range(4):
            P.emit("gpsimd", lambda e, mc=mc: e.tensor_mul(
                out=g.x2[:, 4 + mc, 0:TT], in0=k_sb[:, mc, :], in1=k_sb[:, mc, :]))
        sq_done = P.inc("gpsimd", pv)
        bss = P.banks[3]
        P.wait("tensor", pv, sq_done)
        rs_done0 = None
        for part in range(2):
            if bss.guard is not None:
                P.wait("tensor", bss.guard[0], bss.guard[1])
                bss.guard = None
            for mc in range(4):
                P.emit("tensor", lambda e, part=part, mc=mc: e.matmul(
                    bss.t[0:8, 0:TT], hdsel[:, mc, :], g.x2[:, part * 4 + mc, 0:TT],
                    start=(mc == 0), stop=(mc == 3)))
            ss_done = P.inc("tensor", mm)
            P.wait("scalar", mm, ss_done)
            P.emit("scalar", lambda e: e.activation(
                out=g.lnt[0:8, 0:TT], in_=bss.t[0:8, 0:TT],
                func=AF.Ln, bias=0.0, scale=1.0))
            P.emit("scalar", lambda e: e.activation(
                out=g.lnt[0:8, 0:TT], in_=g.lnt[0:8, 0:TT],
                func=AF.Exp, bias=0.0, scale=-0.5))
            rs_done0 = P.inc("scalar", av)
            bss.guard = (av, rs_done0)
            P.wait("vector", av, rs_done0)
            P.emit("vector", lambda e, part=part: e.tensor_scalar(
                out=rs16[0:8, part, :], in0=g.lnt[0:8, 0:TT],
                scalar1=1.0e12, scalar2=None, op0=ALU.min))
            mn = P.inc("vector", ev)
            P.wait("scalar", ev, mn)
        rs_done = ev.val
        P.wait("vector", tld, tld_v)
        P.wait("gpsimd", tld, tld_v)
        for part, src_sb, dst_rot in ((0, q_sb, qrot), (1, k_sb, None)):
            ct, st_ = (0, 1) if part == 0 else (2, 3)
            for mc in range(4):
                brs = P.banks[4]
                if brs.guard is not None:
                    P.wait("tensor", brs.guard[0], brs.guard[1])
                    brs.guard = None
                P.wait("tensor", ev, rs_done)
                P.emit("tensor", lambda e, part=part, mc=mc, brs=brs: e.matmul(
                    brs.t[:, 0:TT], bcsel[:, mc, :], rs16[0:8, part, :],
                    start=True, stop=True))
                rsb_done = P.inc("tensor", mm)
                P.wait("vector", mm, rsb_done)
                for (o, ii) in ((0, 32), (32, 0), (64, 96), (96, 64)):
                    P.emit("vector", lambda e, o=o, ii=ii, mc=mc, src_sb=src_sb:
                           e.tensor_copy(out=qp_st[o:o + 32, :], in_=src_sb[ii:ii + 32, mc, :]))
                P.emit("vector", lambda e, mc=mc, src_sb=src_sb, brs=brs, part=part:
                       e.scalar_tensor_tensor(
                    out=s1[:, :], in0=src_sb[:, mc, :], scalar=qs2[:, 2 * part:2 * part + 1],
                    in1=brs.t[:, 0:TT], op0=ALU.mult, op1=ALU.mult))
                P.emit("vector", lambda e, brs=brs, part=part: e.scalar_tensor_tensor(
                    out=s2[:, :], in0=qp_st[:, :], scalar=qs2[:, 2 * part + 1:2 * part + 2],
                    in1=brs.t[:, 0:TT], op0=ALU.mult, op1=ALU.mult))
                ee = P.inc("vector", ev)
                brs.guard = (ev, ee)
                P.wait("gpsimd", ev, ee)
                P.emit("gpsimd", lambda e, ct=ct: e.tensor_mul(
                    out=s1[:, :], in0=s1[:, :], in1=rtab[:, ct, :]))
                P.emit("gpsimd", lambda e, st_=st_: e.tensor_mul(
                    out=s2[:, :], in0=s2[:, :], in1=rtab[:, st_, :]))
                t_done = P.inc("gpsimd", pv)
                P.wait("vector", pv, t_done)
                if part == 0:
                    P.emit("vector", lambda e, mc=mc: e.tensor_add(
                        out=qrot[:, mc, :], in0=s1[:, :], in1=s2[:, :]))
                else:
                    P.emit("vector", lambda e, mc=mc, i=i: e.tensor_add(
                        out=kring[:, mc, i % 3, :], in0=s1[:, :], in1=s2[:, :]))
                P.inc("vector", ev)
        rt_ready = P.sem('pv').val
        for mc in range(4):
            m = 8 + mc
            bank = P.banks[2]
            mmv = P.mm_group(bank, [(bank.t[:, 0:TT], wqkv[:, c, m * 128:(m + 1) * 128],
                                     shifted(xt.b16, p, c)) for c in range(8)])
            P.wait("vector", mm, mmv)
            P.emit("vector", lambda e, bank=bank: e.tensor_copy(
                out=vstage[:, :], in_=bank.t[:, 0:TT]))
            vs_done = P.inc("vector", ev)
            bank.guard = (ev, vs_done)
            bt = P.bank_t
            if bt.guard is not None:
                P.wait("tensor", bt.guard[0], bt.guard[1])
                bt.guard = None
            P.wait("tensor", ev, vs_done)
            for b in range(4):
                P.emit("tensor", lambda e, b=b: e.transpose(
                    bt.t[:, b * 128:(b + 1) * 128], vstage[:, b * 128:(b + 1) * 128],
                    g.ident[:, :]))
            tv = P.inc("tensor", mm)
            P.wait("vector", mm, tv)
            for b in range(4):
                P.emit("vector", lambda e, b=b, i=i, mc=mc: e.tensor_copy(
                    out=vtring[:, (4 * i + b) % 12, mc, :], in_=bt.t[:, b * 128:(b + 1) * 128]))
            bt.guard = (ev, P.inc("vector", ev))
        return ev.val

    def score_pieces(j, s):
        if s == 0:
            return [((j - 1) % 3, 0, 512), (j % 3, 0, 128)]
        return [((j - 1) % 3, s * 128, 512), (j % 3, 0, (s + 1) * 128)]

    def emit_attn(j, kv_ready):
        p = j % xt.nbuf
        P.wait("tensor", ev, kv_ready)
        prev_exp = [0, 0]
        prev_ptr = [0, 0]
        for s in range(4):
            cls = s if j == 0 else 4
            for h in range(8):
                unit = s * 8 + h
                par = unit % 2
                qb = 64 * (h % 2)
                hc = h // 2
                bA = P.banks[2 * par]
                bB = P.banks[2 * par + 1]
                for b in (bA, bB):
                    if b.guard is not None:
                        P.wait("tensor", b.guard[0], b.guard[1])
                        b.guard = None
                off = 0
                for (slot, c0, c1) in score_pieces(j, s):
                    while c0 < c1:
                        take = min(c1 - c0, (512 - off) if off < 512 else (640 - off))
                        tgt = bA.t[:, off:off + take] if off < 512 else \
                            bB.t[:, off - 512:off - 512 + take]
                        P.emit("tensor", lambda e, tgt=tgt, qb=qb, hc=hc, s=s, slot=slot,
                               c0=c0, take=take, j=j: e.matmul(
                            tgt, qrot[qb:qb + 64, hc, s * 128:(s + 1) * 128],
                            kring[qb:qb + 64, hc, slot, c0:c0 + take],
                            start=True, stop=True))
                        c0 += take
                        off += take
                sc_done = P.inc("tensor", mm)
                P.wait("vector", mm, sc_done)
                P.wait("vector", av, prev_exp[par])
                P.emit("vector", lambda e, bA=bA, cls=cls, par=par: e.tensor_tensor(
                    out=sc32[:, par, 0:512], in0=bA.t[:, 0:512], in1=g.maskB[:, cls, 0:512],
                    op=ALU.add))
                P.emit("vector", lambda e, bB=bB, cls=cls, par=par: e.tensor_tensor(
                    out=sc32[:, par, 512:640], in0=bB.t[:, 0:128],
                    in1=g.maskB[:, cls, 512:640], op=ALU.add))
                mk = P.inc("vector", ev)
                bA.guard = (ev, mk)
                bB.guard = (ev, mk)
                P.emit("vector", lambda e, par=par: e.reduce_max(
                    out=small[:, 4 * par:4 * par + 1], in_=sc32[:, par, :], axis=AX.X))
                P.emit("vector", lambda e, par=par: e.tensor_scalar(
                    out=small[:, 4 * par + 1:4 * par + 2], in0=small[:, 4 * par:4 * par + 1],
                    scalar1=-1.0, scalar2=None, op0=ALU.mult))
                mx_done = P.inc("vector", ev)
                P.wait("scalar", ev, mx_done)
                P.wait("scalar", mm, prev_ptr[par])
                P.emit("scalar", lambda e, par=par: e.activation(
                    out=pexp[:, par, :], in_=sc32[:, par, :], func=AF.Exp,
                    bias=small[:, 4 * par + 1:4 * par + 2], scale=1.0,
                    accum_out=small[:, 4 * par + 2:4 * par + 3]))
                prev_exp[par] = P.inc("scalar", av)
                P.wait("vector", av, prev_exp[par])
                P.emit("vector", lambda e, par=par: e.reciprocal(
                    out=small[:, 4 * par + 3:4 * par + 4],
                    in_=small[:, 4 * par + 2:4 * par + 3]))
                rc_done = P.inc("vector", ev)
                P.wait("scalar", ev, rc_done)
                P.emit("scalar", lambda e, par=par: e.mul(
                    out=pexp[:, par, :], in_=pexp[:, par, :],
                    mul=small[:, 4 * par + 3:4 * par + 4]))
                norm_done = P.inc("scalar", av)
                bt = P.bank_t
                if bt.guard is not None:
                    P.wait("tensor", bt.guard[0], bt.guard[1])
                    bt.guard = None
                P.wait("tensor", av, norm_done)
                for u in range(5):
                    P.emit("tensor", lambda e, u=u, par=par: e.transpose(
                        bt.t[:, u * 128:(u + 1) * 128],
                        pexp[:, par, u * 128:(u + 1) * 128], g.ident[:, :]))
                prev_ptr[par] = P.inc("tensor", mm)
                P.wait("vector", mm, prev_ptr[par])
                for u in range(5):
                    P.emit("vector", lambda e, u=u, par=par: e.tensor_copy(
                        out=ptr[:, par, u, :], in_=bt.t[:, u * 128:(u + 1) * 128]))
                ptr_done = P.inc("vector", ev)
                bt.guard = (ev, ptr_done)
                avb = P.banks[4]
                if avb.guard is not None:
                    P.wait("tensor", avb.guard[0], avb.guard[1])
                    avb.guard = None
                P.wait("tensor", ev, ptr_done)
                for u in range(5):
                    blk = (4 * (j - 1) + s + u) % 12
                    P.emit("tensor", lambda e, u=u, blk=blk, hc=hc, qb=qb, par=par: e.matmul(
                        avb.t[0:64, 0:128], vtring[:, blk, hc, qb:qb + 64],
                        ptr[:, par, u, :], start=(u == 0), stop=(u == 4)))
                av_done = P.inc("tensor", mm)
                P.wait("vector", mm, av_done)
                P.emit("vector", lambda e, h=h, s=s, qb=qb, hc=hc: e.tensor_copy(
                    out=o_sb[qb:qb + 64, hc, s * 128:(s + 1) * 128], in_=avb.t[0:64, 0:128]))
                avb.guard = (ev, P.inc("vector", ev))
        o_done = ev.val
        P.wait("tensor", ev, o_done)
        st_hist = []
        for m in range(8):
            bank = P.banks[3 + m % 2]
            mmv = P.mm_group(bank, [(bank.t[:, 0:TT], wo[:, c, m * 128:(m + 1) * 128],
                                     o_sb[:, c, :]) for c in range(4)])
            P.wait("vector", mm, mmv)
            if len(st_hist) >= 2:
                P.wait("vector", ssem, st_hist[-2])
            P.emit("vector", lambda e, m=m, bank=bank, p=p: e.scalar_tensor_tensor(
                out=xnst[:, m % 2, :], in0=bank.t[:, 0:TT], scalar=1.0,
                in1=xt.f32[p][:, m, HALO:HALO + TT], op0=ALU.mult, op1=ALU.add))
            bank.guard = (ev, P.inc("vector", ev))
            P.wait("sync", ev, ev.val)
            P.dma("sync", dst[m * 128:(m + 1) * 128, j * TT:(j + 1) * TT],
                  xnst[:, m % 2, :], ssem)
            st_hist.append(ssem.val)
        xt.release(j, ev, ev.val)

    for j in range(NT):
        xt.load(j)
        kv_ready = emit_qkv(j)
        emit_attn(j, kv_ready)
        if j + 1 < NT:
            xt.load(j + 1)
    P.wait("sync", ssem, ssem.val)
    P.stage_end()



def emit_stage_B_ffn(P, g, src, dst, wd, l):
    S, NT = g.S, g.NT
    P.stage_begin()
    xt = XTiles(P, g, src, want_f32=True, want_bf16=True, nbuf=2)
    xhat = P.salloc("bf_xhat", [128, 8, TT], BF16)
    a_sb = P.salloc("bf_a", [128, 22, TT], BF16)
    gsc = P.salloc("bf_g", [128, TT], BF16)
    xnst = P.salloc("bf_xnst", [128, 2, TT], F32)
    lg = P.salloc("bf_lg", [128, 8], F32)
    lb = P.salloc("bf_lb", [128, 8], F32)
    b1 = P.salloc("bf_b1", [128, 44], F32)
    b2 = P.salloc("bf_b2", [128, 8], F32)
    ws1 = WStream(P, wd[f"B{l}_w1"], 44, 8)
    ws2 = WStream(P, wd[f"B{l}_w2"], 8, 22, nslots=3)
    ev, av, mm, pv = P.sem("ev"), P.sem("av"), P.sem("mm"), P.sem("pv")
    ssem = P.sem("bfst")
    wld = P.sem("bfwld")
    P.dma("sync", lg[:, :], wd[f"B{l}_lg"][:, :], wld)
    P.dma("sync", lb[:, :], wd[f"B{l}_lb"][:, :], wld)
    P.dma("sync", b1[:, :], wd[f"B{l}_b1"][:, :], wld)
    P.dma("sync", b2[:, :], wd[f"B{l}_b2"][:, :], wld)
    for e in ("tensor", "vector", "scalar"):
        P.wait(e, wld, wld.val)

    for i in range(NT):
        xt.load(i)
        for k in range(3):
            ws1.load(k)
        p = i % xt.nbuf
        xf = lambda c, p=p: xt.f32[p][:, c, (HALO if c < 4 else HALO - 1):
                                      (HALO if c < 4 else HALO - 1) + TT]
        xb = lambda c, p=p: xt.b16[p][:, c, (HALO if c < 4 else HALO - 1):
                                      (HALO if c < 4 else HALO - 1) + TT]
        for e in ("scalar", "tensor", "vector"):
            xt.wait_tile(e, i)
        ln_done = emit_layernorm(P, g, xf, xb, TT, xhat, affine=(lg, lb))
        P.wait("tensor", ev, ln_done)
        prev_mul = 0
        for m in range(44):
            ws1.load(m + 3)
            w = ws1.use(m)
            bank = P.banks[m % 3]
            mmv = P.mm_group(bank, [(bank.t[:, 0:TT], w[:, c, :], xhat[:, c, :])
                                    for c in range(8)])
            ws1.done(m)
            if m < 22:
                P.wait("vector", mm, mmv)
                P.emit("vector", lambda e, m=m, bank=bank: e.tensor_scalar(
                    out=a_sb[:, m, :], in0=bank.t[:, 0:TT], scalar1=b1[:, m:m + 1],
                    scalar2=None, op0=ALU.add))
                bank.guard = (ev, P.inc("vector", ev))
            else:
                P.wait("scalar", mm, mmv)
                P.wait("scalar", pv, prev_mul)
                P.emit("scalar", lambda e, m=m, bank=bank: e.activation(
                    out=gsc[:, :], in_=bank.t[:, 0:TT], func=AF.Gelu,
                    bias=b1[:, m:m + 1], scale=1.0))
                gd = P.inc("scalar", av)
                bank.guard = (av, gd)
                P.wait("gpsimd", av, gd)
                P.emit("gpsimd", lambda e, mc=m - 22: e.tensor_mul(
                    out=a_sb[:, mc, :], in0=a_sb[:, mc, :], in1=gsc[:, :]))
                prev_mul = P.inc("gpsimd", pv)
        u_done = prev_mul
        for k in range(2):
            ws2.load(k)
        P.wait("tensor", pv, u_done)
        st_hist = []
        for m in range(8):
            ws2.load(m + 2)
            w = ws2.use(m)
            bank = P.banks[m % 3]
            mmv = P.mm_group(bank, [(bank.t[:, 0:TT], w[:, c, :], a_sb[:, c, :])
                                    for c in range(22)])
            ws2.done(m)
            P.wait("vector", mm, mmv)
            if len(st_hist) >= 2:
                P.wait("vector", ssem, st_hist[-2])
            P.emit("vector", lambda e, m=m, bank=bank, p=p: e.scalar_tensor_tensor(
                out=xnst[:, m % 2, :], in0=bank.t[:, 0:TT], scalar=b2[:, m:m + 1],
                in1=xt.f32[p][:, m, HALO:HALO + TT], op0=ALU.add, op1=ALU.add))
            bank.guard = (ev, P.inc("vector", ev))
            P.wait("sync", ev, ev.val)
            P.dma("sync", dst[m * 128:(m + 1) * 128, i * TT:(i + 1) * TT],
                  xnst[:, m % 2, :], ssem)
            st_hist.append(ssem.val)
        xt.release(i, ev, ev.val)
        ws1.loaded = -1
        ws2.loaded = -1
    P.wait("sync", ssem, ssem.val)
    P.stage_end()



def emit_stage_D1(P, g, src, hs32, hs16, wd):
    S, NT = g.S, g.NT
    P.stage_begin()
    xt = XTiles(P, g, src, want_f32=True, want_bf16=True, nbuf=2)
    xhat = P.salloc("d_xhat", [128, 8, TT], BF16)
    hbuf = P.salloc("d_h", [128, 16, TT + 3], BF16)
    gate = P.salloc("d_gate", [128, 16, TT], BF16)
    h2st = P.salloc("d_h2st", [128, 2, TT], F32)
    convw = P.salloc("d_convw", [128, 4, 8, 2, 256], BF16)
    convb = P.salloc("d_convb", [128, 16], F32)
    ws_ip = WStream(P, wd["D_inproj"], 32, 8)
    ev, av, mm, pv = P.sem("ev"), P.sem("av"), P.sem("mm"), P.sem("pv")
    ssem = P.sem("dst1")
    wld = P.sem("dwld")
    for tap in range(4):
        for gg in range(8):
            for kc in range(2):
                P.dma("gpsimd", convw[:, tap, gg, kc, :],
                      wd["D_conv"][tap, gg, kc * 128:(kc + 1) * 128, :], wld)
    P.dma("sync", convb[:, :], wd["D_convb"][:, :], wld)
    P.wait("tensor", wld, wld.val)

    hist_guard = 0
    for i in range(NT):
        xt.load(i)
        for k in range(3):
            ws_ip.load(k)
        p = i % xt.nbuf
        xf = lambda c, p=p: xt.f32[p][:, c, HALO:HALO + TT]
        xb = lambda c, p=p: xt.b16[p][:, c, HALO:HALO + TT]
        for e in ("scalar", "tensor", "vector"):
            xt.wait_tile(e, i)
        ln_done = emit_layernorm(P, g, xf, xb, TT, xhat)
        P.wait("gpsimd", mm, hist_guard)
        if i == 0:
            P.emit("gpsimd", lambda e: e.memset(hbuf[:, :, 0:3], 0.0))
        else:
            P.emit("gpsimd", lambda e: e.tensor_copy(out=hbuf[:, :, 0:3],
                                                     in_=hbuf[:, :, TT:TT + 3]))
        hist_done = P.inc("gpsimd", pv)
        P.wait("tensor", ev, ln_done)
        for m in range(32):
            ws_ip.load(m + 3)
            w = ws_ip.use(m)
            bank = P.banks[m % 3]
            mmv = P.mm_group(bank, [(bank.t[:, 0:TT], w[:, c, :], xhat[:, c, :])
                                    for c in range(8)])
            ws_ip.done(m)
            if m < 16:
                if m % 2 == 0:
                    P.wait("vector", mm, mmv)
                    P.wait("vector", pv, hist_done)
                    P.emit("vector", lambda e, m=m, bank=bank: e.tensor_copy(
                        out=hbuf[:, m, 3:3 + TT], in_=bank.t[:, 0:TT]))
                    bank.guard = (ev, P.inc("vector", ev))
                else:
                    P.wait("scalar", mm, mmv)
                    P.wait("scalar", pv, hist_done)
                    P.emit("scalar", lambda e, m=m, bank=bank: e.copy(
                        out=hbuf[:, m, 3:3 + TT], in_=bank.t[:, 0:TT]))
                    bank.guard = (av, P.inc("scalar", av))
            else:
                P.wait("scalar", mm, mmv)
                P.emit("scalar", lambda e, m=m, bank=bank: e.activation(
                    out=gate[:, m - 16, :], in_=bank.t[:, 0:TT], func=AF.Silu,
                    bias=0.0, scale=1.0))
                bank.guard = (av, P.inc("scalar", av))
        h_done_ev, h_done_av = ev.val, av.val
        P.wait("tensor", ev, h_done_ev)
        P.wait("tensor", av, h_done_av)
        st_hist = []
        for mc in range(16):
            gg, hc = mc // 2, mc % 2
            bank = P.banks[mc % 3]
            mms = []
            for kc in range(2):
                for tap in range(4):
                    mms.append((bank.t[:, 0:TT],
                                convw[:, tap, gg, kc, hc * 128:(hc + 1) * 128],
                                hbuf[:, gg * 2 + kc, tap:tap + TT]))
            mmv = P.mm_group(bank, mms)
            P.wait("vector", mm, mmv)
            if len(st_hist) >= 2:
                P.wait("vector", ssem, st_hist[-2][0])
                P.wait("vector", P.sem("dst1b"), st_hist[-2][1])
            P.emit("vector", lambda e, mc=mc, bank=bank: e.scalar_tensor_tensor(
                out=h2st[:, mc % 2, :], in0=bank.t[:, 0:TT], scalar=convb[:, mc:mc + 1],
                in1=gate[:, mc, :], op0=ALU.add, op1=ALU.mult))
            bank.guard = (ev, P.inc("vector", ev))
            P.wait("sync", ev, ev.val)
            P.wait("gpsimd", ev, ev.val)
            P.dma("sync", hs32[mc * 128:(mc + 1) * 128, i * TT:(i + 1) * TT],
                  h2st[:, mc % 2, :], ssem)
            P.dma("gpsimd", hs16[mc * 128:(mc + 1) * 128, i * TT:(i + 1) * TT],
                  h2st[:, mc % 2, :], P.sem("dst1b"))
            st_hist.append((ssem.val, P.sem("dst1b").val))
        hist_guard = mm.val
        xt.release(i, ev, ev.val)
        ws_ip.loaded = -1
    P.wait("sync", ssem, ssem.val)
    P.wait("gpsimd", P.sem("dst1b"), P.sem("dst1b").val)
    P.stage_end()



def emit_stage_D2(P, g, hs32, hs16, xsrc, dst, wd):
    S, NT = g.S, g.NT
    P.stage_begin()
    ht = XTiles(P, g, hs32, nchunks=16, want_f32=True, want_bf16=True, nbuf=1,
                src16=hs16)
    xres = XTiles(P, g, xsrc, want_f32=True, want_bf16=False, nbuf=1)
    xhat2 = P.salloc("d2_xhat", [128, 16, TT], BF16)
    m1 = P.salloc("d2_m1", [128, 8, TT], BF16)
    h3 = P.salloc("d2_h3", [128, 16, TT], BF16)
    xnst = P.salloc("d2_xnst", [128, 2, TT], F32)
    fcw = P.salloc("d2_fcw", [128, 16, 1024], BF16)
    fcb = P.salloc("d2_fcb", [128, 8], F32)
    lg = P.salloc("d2_lg", [128, 16], F32)
    lb = P.salloc("d2_lb", [128, 16], F32)
    cpb = P.salloc("d2_cpb", [128, 16], F32)
    ws_cp = WStream(P, wd["D_cproj"], 16, 8)
    ws_ow = WStream(P, wd["D_outw"], 8, 16, nslots=3)
    ev, av, mm = P.sem("ev"), P.sem("av"), P.sem("mm")
    ssem = P.sem("dst2")
    wld = P.sem("d2wld")
    for c in range(16):
        P.dma("gpsimd", fcw[:, c, :], wd["D_fc"][c * 128:(c + 1) * 128, :], wld)
    P.dma("sync", fcb[:, :], wd["D_fcb"][:, :], wld)
    P.dma("sync", lg[:, :], wd["D_lg"][:, :], wld)
    P.dma("sync", lb[:, :], wd["D_lb"][:, :], wld)
    P.dma("sync", cpb[:, :], wd["D_cprojb"][:, :], wld)
    P.wait("tensor", wld, wld.val)
    P.wait("scalar", wld, wld.val)
    P.wait("vector", wld, wld.val)

    for i in range(NT):
        ht.load(i)
        xres.load(i)
        for k in range(3):
            ws_cp.load(k)
        p = i % ht.nbuf
        hf = lambda c, p=p: ht.f32[p][:, c, HALO:HALO + TT]
        hb = lambda c, p=p: ht.b16[p][:, c, HALO:HALO + TT]
        for e in ("scalar", "tensor", "vector"):
            ht.wait_tile(e, i)
        ln_done = emit_layernorm(P, g, hf, hb, TT, xhat2, nchunks=16, affine=(lg, lb))
        P.wait("tensor", ev, ln_done)
        for m in range(8):
            bank = P.banks[m % 3]
            mmv = P.mm_group(bank, [(bank.t[:, 0:TT], fcw[:, c, m * 128:(m + 1) * 128],
                                     xhat2[:, c, :]) for c in range(16)])
            P.wait("scalar", mm, mmv)
            P.emit("scalar", lambda e, m=m, bank=bank: e.activation(
                out=m1[:, m, :], in_=bank.t[:, 0:TT], func=AF.Silu,
                bias=fcb[:, m:m + 1], scale=1.0))
            bank.guard = (av, P.inc("scalar", av))
        fc_done = av.val
        P.wait("tensor", av, fc_done)
        for mc in range(16):
            ws_cp.load(mc + 3)
            w = ws_cp.use(mc)
            bank = P.banks[mc % 3]
            mmv = P.mm_group(bank, [(bank.t[:, 0:TT], w[:, c, :], m1[:, c, :])
                                    for c in range(8)])
            ws_cp.done(mc)
            P.wait("vector", mm, mmv)
            P.emit("vector", lambda e, mc=mc, bank=bank, p=p: e.scalar_tensor_tensor(
                out=h3[:, mc, :], in0=bank.t[:, 0:TT], scalar=cpb[:, mc:mc + 1],
                in1=ht.f32[p][:, mc, HALO:HALO + TT], op0=ALU.add, op1=ALU.add))
            bank.guard = (ev, P.inc("vector", ev))
        h3_done = ev.val
        for k in range(2):
            ws_ow.load(k)
        P.wait("tensor", ev, h3_done)
        xres.wait_tile("vector", i)
        st_hist = []
        for m in range(8):
            ws_ow.load(m + 2)
            w = ws_ow.use(m)
            bank = P.banks[m % 3]
            mmv = P.mm_group(bank, [(bank.t[:, 0:TT], w[:, c, :], h3[:, c, :])
                                    for c in range(16)])
            ws_ow.done(m)
            P.wait("vector", mm, mmv)
            if len(st_hist) >= 2:
                P.wait("vector", ssem, st_hist[-2])
            xp = i % xres.nbuf
            P.emit("vector", lambda e, m=m, bank=bank, xp=xp: e.scalar_tensor_tensor(
                out=xnst[:, m % 2, :], in0=bank.t[:, 0:TT], scalar=1.0,
                in1=xres.f32[xp][:, m, HALO:HALO + TT], op0=ALU.mult, op1=ALU.add))
            bank.guard = (ev, P.inc("vector", ev))
            P.wait("sync", ev, ev.val)
            P.dma("sync", dst[m * 128:(m + 1) * 128, i * TT:(i + 1) * TT],
                  xnst[:, m % 2, :], ssem)
            st_hist.append(ssem.val)
        ht.release(i, ev, ev.val)
        xres.release(i, ev, ev.val)
        ws_cp.loaded = -1
        ws_ow.loaded = -1
    P.wait("sync", ssem, ssem.val)
    P.stage_end()



def _bw(l):
    return ((f"B{l}_wqkv", [1024, 1536], F32),
            (f"B{l}_wo", [512, 1024], F32), (f"B{l}_qs4", [128, 4], F32),
            (f"B{l}_lg", [128, 8], F32), (f"B{l}_lb", [128, 8], F32),
            (f"B{l}_w1", [44, 1024, 128], F32), (f"B{l}_b1", [128, 44], F32),
            (f"B{l}_w2", [8, 2816, 128], F32), (f"B{l}_b2", [128, 8], F32))


STAGE_WEIGHTS = {
    "B0a": _bw(0), "B0f": _bw(0), "B1a": _bw(1), "B1f": _bw(1),
    "A": (("A_wq", [1024, 1024], F32), ("A_wk", [1024, 128], F32),
          ("A_wv", [1024, 128], F32), ("A_wo", [1024, 1024], F32),
          ("A_bq", [128, 8], F32), ("A_bk", [128, 1], F32),
          ("A_bv", [128, 1], F32), ("A_bo", [128, 8], F32)),
    "E": (("E_wg", [24, 1024, 128], I32), ("E_wu", [24, 1024, 128], I32),
          ("E_wd", [8, 3072, 128], I32)),
    "D": (("D_inproj", [32, 1024, 128], F32), ("D_conv", [4, 8, 256, 256], F32),
          ("D_convb", [128, 16], F32), ("D_lg", [128, 16], F32), ("D_lb", [128, 16], F32),
          ("D_fc", [2048, 1024], F32), ("D_fcb", [128, 8], F32),
          ("D_cproj", [16, 1024, 128], F32), ("D_cprojb", [128, 16], F32),
          ("D_outw", [8, 2048, 128], F32)),
}


def build(S, stages):
    nc = bass.Bass("TRN2", target_bir_lowering=False)
    P = Prog(nc)
    P.make_banks()
    td = host_tables(S)
    x_in = nc.dram_tensor("x", [D, S], F32, kind="ExternalInput").ap()
    y_out = nc.dram_tensor("y", [D, S], F32, kind="ExternalOutput").ap()
    xs = [nc.dram_tensor(f"xs{i}", [D, S], F32, kind="Internal").ap() for i in range(4)]
    hs32 = hs16 = None
    if "D" in stages:
        hs32 = nc.dram_tensor("hs32", [2048, S], F32, kind="Internal").ap()
        hs16 = nc.dram_tensor("hs16", [2048, S], BF16, kind="Internal").ap()
    wd = {}
    wnames = []
    for st in stages:
        for nm, shape, dt in STAGE_WEIGHTS.get(st, ()):
            if nm not in wd:
                wd[nm] = nc.dram_tensor(nm, shape, dt, kind="ExternalInput").ap()
                wnames.append(nm)
    g = setup_globals(P, S, td)

    cur = x_in
    bsave = None
    navail = list(range(4))
    cur_idx = None

    def fresh():
        return xs[navail.pop(0)]

    def recycle(ap):
        for i in range(4):
            if xs[i] is ap:
                navail.append(i)

    for si, st in enumerate(stages):
        last = si == len(stages) - 1
        dst = y_out if last else fresh()
        if st == "E":
            emit_stage_E(P, g, cur, dst, wd)
        elif st == "A":
            emit_stage_A(P, g, cur, dst, wd)
        elif st == "Bpre":
            emit_stage_LN(P, g, cur, dst)
            bsave = cur
        elif st == "Badd":
            emit_stage_ADD(P, g, bsave, cur, dst)
            if bsave is not x_in:
                recycle(bsave)
            bsave = None
        elif st in ("B0a", "B1a"):
            emit_stage_B_attn(P, g, cur, dst, wd, int(st[1]))
        elif st in ("B0f", "B1f"):
            emit_stage_B_ffn(P, g, cur, dst, wd, int(st[1]))
        elif st == "D":
            emit_stage_D1(P, g, cur, hs32, hs16, wd)
            P.barrier()
            emit_stage_D2(P, g, hs32, hs16, cur, dst, wd)
        else:
            raise ValueError(st)
        P.barrier()
        if cur is not x_in and cur is not bsave:
            recycle(cur)
        cur = dst
    P.finalize()
    return nc, wnames


def run_stages(x_cores, weights, tables, S, stages, core_ids=None):
    nc, wnames = build(S, stages)
    in_maps = []
    for xc in x_cores:
        m = {"x": xc}
        m.update(tables)
        for nm in wnames:
            m[nm] = weights[nm]
        in_maps.append(m)
    ids = core_ids if core_ids is not None else list(range(len(x_cores)))
    res = run_bass_kernel_spmd(nc, in_maps, core_ids=ids)
    return [np.asarray(r["y"], np.float32) for r in res.results]


def kernel(**inputs):
    S = 8192
    x = inputs["L0_attn_x"]
    w = host_weights(inputs)
    td = host_tables(S)
    x_cores = [np.ascontiguousarray(x[c].T) for c in range(8)]
    outs = run_stages(x_cores, w, td, S,
                      ("A", "Bpre", "B0a", "B0f", "B1a", "B1f", "Badd", "D", "E"))
    out = np.empty((8, 8192, 1024), np.float32)
    for c in range(8):
        out[c] = outs[c].T
    return out
